# version 02
# baseline (speedup 1.0000x reference)
import numpy as np
import concourse.bass as bass
import concourse.mybir as mybir
from concourse.bass_utils import run_bass_kernel_spmd
from concourse.aot_env import allow_ici_all_to_all

F32 = mybir.dt.float32
BF16 = mybir.dt.bfloat16
I32 = mybir.dt.int32
U8 = mybir.dt.uint8
AF = mybir.ActivationFunctionType
ALU = mybir.AluOpType
AXX = mybir.AxisListType.X

NC8 = 8
T = 8192
D = 2048
DI = 4096
DFF = 8192
TL = 1024
HALO = 128
TH = TL + HALO
EPS = 1e-5


class P:

    def __init__(self, nc, stack):
        self.nc = nc
        self.stack = stack
        self.ops = []
        self.phase = 0
        self.cnt = 0
        self.sems = [stack.enter_context(nc.semaphore("ph0"))]

    def new_phase(self):
        self.phase += 1
        self.cnt = 0
        self.sems.append(self.stack.enter_context(self.nc.semaphore(f"ph{self.phase}")))

    def add(self, eng, fn, dma=False):
        self.ops.append((eng, fn, 16 if dma else 1, self.phase, self.cnt))
        self.cnt += 16 if dma else 1

    def dma(self, eng, out, in_):
        self.add(eng, lambda e: e.dma_start(out=out, in_=in_), dma=True)

    def dmaT(self, eng, out, in_):
        self.add(eng, lambda e: e.dma_start_transpose(out, in_), dma=True)

    def emit(self, block):
        seq = self.ops
        sems = self.sems
        fph, fcnt = self.phase, self.cnt

        def mk(name):
            def body(e):
                for i, (eng, fn, inc, ph, before) in enumerate(seq):
                    if eng != name:
                        continue
                    if i > 0:
                        _, _, pinc, pph, pbefore = seq[i - 1]
                        e.wait_ge(sems[pph], pbefore + pinc)
                    fn(e).then_inc(sems[ph], inc)
                if name == "gpsimd":
                    e.wait_ge(sems[fph], fcnt)
            return body

        block.sync(mk("sync"))
        block.scalar(mk("scalar"))
        block.vector(mk("vector"))
        block.tensor(mk("tensor"))
        block.gpsimd(mk("gpsimd"))


def build_program(debug=False):
    import contextlib
    nc = bass.Bass("TRN2", target_bir_lowering=False, debug=False)

    def inp(name, shape, dt=F32):
        return nc.dram_tensor(name, list(shape), dt, kind="ExternalInput").ap()

    xs = inp("xs", [TH, D])
    win_t = inp("win_t", [64, 128, 2048])
    cw = inp("cw", [32, 128, 4])
    cb = inp("cb", [32, 128, 1])
    wx_t = inp("wx_t", [32, 128, 160])
    wdt = inp("wdt", [4, 128, 128])
    dtb = inp("dtb", [4, 128, 1])
    Ain = inp("Ain", [4, 128, 16])
    Din = inp("Din", [32, 128, 1])
    wout_t = inp("wout_t", [32, 128, 2048])
    w1_t = inp("w1_t", [64, 128, 2048])
    w2_t = inp("w2_t", [64, 128, 2048])
    cvec = inp("cvec", [1, D])
    wmod_j = inp("wmod_j", [16, 128, 1536])
    mask3 = inp("mask3", [128, 3])

    out = nc.dram_tensor("out", [TL, D], F32, kind="ExternalOutput").ap()

    drh = {}

    def idram(name, shape, dt=F32):
        drh[name] = nc.dram_tensor(name, list(shape), dt)
        return drh[name].ap()

    agm_in = idram("agm_in", [1, 1536])
    agm_out = idram("agm_out", [8, 1536])
    a2a_in = idram("a2a_in", [8, 4, 128, TL], BF16)
    a2a_out = idram("a2a_out", [8, 4, 128, TL], BF16)
    ag_in = idram("ag_in", [160, TL])
    ag_out = idram("ag_out", [8, 160, TL])
    a2y_in = idram("a2y_in", [8, 4, 128, TL], BF16)
    a2y_out = idram("a2y_out", [8, 4, 128, TL], BF16)
    sg_dram = idram("sg_dram", [32, 128, TL], BF16)
    x2_dram = idram("x2_dram", [8, 128, D])
    bcrows = idram("bcrows", [32, TL], BF16)
    modv = idram("modv", [1, 12288])
    sc_dram = idram("sc_dram", [1, D])
    rowbf_dram = idram("rowbf_dram", [1, D], BF16)

    st = contextlib.ExitStack()
    p = P(nc, st)

    def sball(stack):
        def f(name, shape, dt=F32):
            return stack.enter_context(nc.sbuf_tensor(name, list(shape), dt))
        return f

    sbp = sball(st)
    a_sl = sbp("a_sl", [128, 64])
    dtb_sl = sbp("dtb_sl", [128, 4])
    d_sl = sbp("d_sl", [128, 32])
    cw_sl = sbp("cw_sl", [128, 128])
    cb_sl = sbp("cb_sl", [128, 32])
    mask_sl = sbp("mask_sl", [128, 3], BF16)
    carry = sbp("carry", [128, 64])
    wdt_sl = sbp("wdt_sl", [128, 512], BF16)
    consts = sbp("consts", [128, 4])

    block = st.enter_context(nc.Block())

    with contextlib.ExitStack() as ph:
        psb = sball(ph)
        f0 = psb("f0", [128, 512])
        modrow = psb("modrow", [1, 12288])
        p.dma("sync", a_sl[:, :].rearrange("p (q s) -> p q s", q=4), Ain[:, :, :].transpose([1, 0, 2]))
        p.dma("sync", dtb_sl[:, :].rearrange("p (q o) -> p q o", q=4), dtb[:, :, :].transpose([1, 0, 2]))
        p.dma("sync", d_sl[:, :].rearrange("p (q o) -> p q o", q=32), Din[:, :, :].transpose([1, 0, 2]))
        p.dma("sync", cw_sl[:, :].rearrange("p (q k) -> p q k", q=32), cw[:, :, :].transpose([1, 0, 2]))
        p.dma("sync", cb_sl[:, :].rearrange("p (q o) -> p q o", q=32), cb[:, :, :].transpose([1, 0, 2]))
        p.dma("sync", f0[:, 0:3], mask3[:, :])
        p.add("vector", lambda e: e.tensor_copy(out=mask_sl[:], in_=f0[:, 0:3]))
        p.add("vector", lambda e: e.memset(carry[:], 0.0))
        p.add("vector", lambda e: e.memset(consts[:, 0:1], EPS))
        p.add("vector", lambda e: e.memset(consts[:, 1:2], 10.0))
        p.add("vector", lambda e: e.memset(consts[:, 2:3], 0.1))
        p.dma("sync", f0[:, 0:512].rearrange("p (q r) -> p q r", q=4), wdt[:, :, :].transpose([1, 0, 2]))
        p.add("vector", lambda e: e.tensor_copy(out=wdt_sl[:], in_=f0[:, 0:512]))

        r0 = psb("r0", [1, D])
        r1 = psb("r1", [1, D])
        r2 = psb("r2", [1, D])
        m0 = psb("m0", [1, D], U8)
        sccols = psb("sccols", [128, 16])
        p.dma("sync", r0[:, :], cvec[:, :])
        p.add("scalar", lambda e: e.activation(out=r1[:], in_=r0[:], func=AF.Silu))
        p.dma("gpsimd", sc_dram[:, :], r1[:, :])
        p.dma("sync", sccols[:, :], sc_dram[0, :].rearrange("(k p) -> p k", p=128))

        wm_f = psb("wm_f", [128, 1536])
        mod_ps = ph.enter_context(nc.psum_tensor("mod_ps", [1, 1536], F32))
        modslice = psb("modslice", [1, 1536])
        for k in range(16):
            p.dma("sync", wm_f[:, :], wmod_j[k, :, :])
            for c in range(3):
                p.add("tensor", lambda e, k=k, c=c: e.matmul(
                    mod_ps[:, c * 512:(c + 1) * 512], sccols[:, k:k + 1],
                    wm_f[:, c * 512:(c + 1) * 512], start=(k == 0), stop=(k == 15)))
        p.add("vector", lambda e: e.tensor_copy(out=modslice[:], in_=mod_ps[:]))
        p.dma("gpsimd", agm_in[:, :], modslice[:, :])
        p.add("gpsimd", lambda e: e.collective_compute(
            "AllGather", ALU.bypass, replica_groups=[list(range(NC8))],
            ins=[drh["agm_in"].ap().opt()], outs=[drh["agm_out"].ap().opt()]))
        p.dma("sync", modrow[:, :], agm_out[:, :].rearrange("a b -> (a b)")[None, :])
        for b0 in (2 * D, 5 * D):
            p.add("vector", lambda e, b0=b0: e.tensor_single_scalar(
                out=m0[:], in_=modrow[:, b0:b0 + D], scalar=0.0, op=ALU.is_ge))
            p.add("vector", lambda e, b0=b0: e.tensor_scalar_max(
                out=r0[:], in0=modrow[:, b0:b0 + D], scalar1=0.25))
            p.add("vector", lambda e, b0=b0: e.tensor_scalar_min(
                out=r1[:], in0=modrow[:, b0:b0 + D], scalar1=-0.25))
            p.add("vector", lambda e, b0=b0: e.select(
                out=modrow[:, b0:b0 + D], mask=m0[:], on_true=r0[:], on_false=r1[:]))
        for b0 in (D, 4 * D):
            p.add("vector", lambda e, b0=b0: e.tensor_scalar_add(
                out=modrow[:, b0:b0 + D], in0=modrow[:, b0:b0 + D], scalar1=1.0))
        p.dma("gpsimd", modv[:, :], modrow[:, :])

    p.new_phase()
    with contextlib.ExitStack() as ph:
        psb = sball(ph)
        ca1b = psb("ca1b", [128, D])
        sab = psb("sab", [128, D])
        p.dma("sync", ca1b[:, :], modv[0, D:2 * D][None, :].broadcast_to([128, D]))
        p.dma("sync", sab[:, :], modv[0, 0:D][None, :].broadcast_to([128, D]))

        hT = [psb(f"hT{i}", [128, TH], BF16) for i in range(16)]
        xtile = psb("xtile", [128, D])
        sqt = psb("sqt", [128, D])
        stat = psb("stat", [128, 4])
        xhat = psb("xhat", [128, D])
        htm = psb("htm", [128, D], BF16)
        for tt in range(9):
            p.dma("sync", xtile[:, :], xs[tt * 128:(tt + 1) * 128, :])
            p.add("vector", lambda e: e.tensor_reduce(
                out=stat[:, 0:1], in_=xtile[:], axis=AXX, op=ALU.add))
            p.add("scalar", lambda e: e.square(out=sqt[:], in_=xtile[:]))
            p.add("vector", lambda e: e.tensor_reduce(
                out=stat[:, 1:2], in_=sqt[:], axis=AXX, op=ALU.add))
            p.add("vector", lambda e: e.tensor_scalar_mul(
                out=stat[:, 2:3], in0=stat[:, 0:1], scalar1=1.0 / D))
            p.add("scalar", lambda e: e.square(out=stat[:, 0:1], in_=stat[:, 2:3]))
            p.add("vector", lambda e: e.scalar_tensor_tensor(
                out=stat[:, 1:2], in0=stat[:, 1:2], scalar=1.0 / D,
                in1=stat[:, 0:1], op0=ALU.mult, op1=ALU.subtract))
            p.add("scalar", lambda e: e.activation(
                out=stat[:, 3:4], in_=stat[:, 1:2], func=AF.Sqrt, bias=consts[:, 0:1]))
            p.add("vector", lambda e: e.reciprocal(
                out=stat[:, 3:4], in_=stat[:, 3:4]))
            p.add("vector", lambda e: e.tensor_scalar(
                out=xhat[:], in0=xtile[:], scalar1=stat[:, 2:3], scalar2=stat[:, 3:4],
                op0=ALU.subtract, op1=ALU.mult))
            p.add("vector", lambda e: e.tensor_mul(out=xhat[:], in0=xhat[:], in1=ca1b[:]))
            p.add("vector", lambda e: e.tensor_add(out=htm[:], in0=xhat[:], in1=sab[:]))
            for di in range(16):
                p.dmaT("sync", hT[di][:, tt * 128:(tt + 1) * 128],
                       htm[:, di * 128:(di + 1) * 128])
        for di in range(16):
            p.add("vector", lambda e, di=di: e.tensor_mul(
                out=hT[di][:, HALO - 3:HALO], in0=hT[di][:, HALO - 3:HALO],
                in1=mask_sl[:]))

        wraw = psb("wraw", [128, 2048])
        wbf = psb("wbf", [128, 2048], BF16)
        ip0 = ph.enter_context(nc.psum_tensor("ip0", [128, 512], F32))
        ip1 = ph.enter_context(nc.psum_tensor("ip1", [128, 512], F32))
        ip2 = ph.enter_context(nc.psum_tensor("ip2", [128, 512], F32))
        ipp = [ip0, ip1, ip2]
        FDC = [(0, 512), (512, 1024), (1024, TH)]
        xh = psb("xh", [128, TH], BF16)
        xc0 = psb("xc0", [128, TL], BF16)
        xc1 = psb("xc1", [128, TL], BF16)
        xt_t = [psb(f"xt{i}", [128, TL], BF16) for i in range(32)]
        sgt = psb("sgt", [128, TL], BF16)

        for et in range(64):
            p.dma("sync", wraw[:, :], win_t[et, :, :])
            p.add("vector", lambda e: e.tensor_copy(out=wbf[:], in_=wraw[:]))
            for k in range(16):
                for ci, (c0, c1) in enumerate(FDC):
                    p.add("tensor", lambda e, k=k, ci=ci, c0=c0, c1=c1: e.matmul(
                        ipp[ci][:, 0:c1 - c0], wbf[:, k * 128:(k + 1) * 128],
                        hT[k][:, c0:c1], start=(k == 0), stop=(k == 15)))
            if et < 32:
                for ci, (c0, c1) in enumerate(FDC):
                    p.add("scalar", lambda e, ci=ci, c0=c0, c1=c1: e.activation(
                        out=xh[:, c0:c1], in_=ipp[ci][:, 0:c1 - c0], func=AF.Copy))
                p.add("vector", lambda e, et=et: e.tensor_scalar_mul(
                    out=xc0[:], in0=xh[:, HALO - 3:HALO - 3 + TL],
                    scalar1=cw_sl[:, 4 * et:4 * et + 1]))
                p.add("vector", lambda e, et=et: e.scalar_tensor_tensor(
                    out=xc1[:], in0=xh[:, HALO - 2:HALO - 2 + TL],
                    scalar=cw_sl[:, 4 * et + 1:4 * et + 2], in1=xc0[:],
                    op0=ALU.mult, op1=ALU.add))
                p.add("vector", lambda e, et=et: e.scalar_tensor_tensor(
                    out=xc0[:], in0=xh[:, HALO - 1:HALO - 1 + TL],
                    scalar=cw_sl[:, 4 * et + 2:4 * et + 3], in1=xc1[:],
                    op0=ALU.mult, op1=ALU.add))
                p.add("vector", lambda e, et=et: e.scalar_tensor_tensor(
                    out=xc1[:], in0=xh[:, HALO:HALO + TL],
                    scalar=cw_sl[:, 4 * et + 3:4 * et + 4], in1=xc0[:],
                    op0=ALU.mult, op1=ALU.add))
                p.add("scalar", lambda e, et=et: e.activation(
                    out=xt_t[et][:], in_=xc1[:], func=AF.Silu,
                    bias=cb_sl[:, et:et + 1]))
                p.dma("gpsimd", a2a_in[et // 4, et % 4, :, :], xt_t[et][:, :])
            else:
                p.add("scalar", lambda e: e.activation(
                    out=sgt[:, 0:384], in_=ip0[:, 128:512], func=AF.Silu))
                p.add("scalar", lambda e: e.activation(
                    out=sgt[:, 384:896], in_=ip1[:, 0:512], func=AF.Silu))
                p.add("scalar", lambda e: e.activation(
                    out=sgt[:, 896:1024], in_=ip2[:, 0:128], func=AF.Silu))
                p.dma("gpsimd", sg_dram[et - 32, :, :], sgt[:, :])

        wxs = psb("wxs", [128, 160])
        wxb = psb("wxb", [128, 160], BF16)
        xp0 = ph.enter_context(nc.psum_tensor("xp0", [128, TL], F32))
        xp1 = ph.enter_context(nc.psum_tensor("xp1", [32, TL], F32))
        ssm_hi = psb("ssm_hi", [128, TL])
        ssm_lo = psb("ssm_lo", [32, TL])
        for k in range(32):
            p.dma("sync", wxs[:, :], wx_t[k, :, :])
            p.add("vector", lambda e: e.tensor_copy(out=wxb[:], in_=wxs[:]))
            for c0 in (0, 512):
                p.add("tensor", lambda e, k=k, c0=c0: e.matmul(
                    xp0[:, c0:c0 + 512], wxb[:, 0:128], xt_t[k][:, c0:c0 + 512],
                    start=(k == 0), stop=(k == 31)))
                p.add("tensor", lambda e, k=k, c0=c0: e.matmul(
                    xp1[:, c0:c0 + 512], wxb[:, 128:160], xt_t[k][:, c0:c0 + 512],
                    start=(k == 0), stop=(k == 31)))
        p.add("vector", lambda e: e.tensor_copy(out=ssm_hi[:], in_=xp0[:]))
        p.add("vector", lambda e: e.tensor_copy(out=ssm_lo[:], in_=xp1[:]))
        p.dma("gpsimd", ag_in[0:128, :], ssm_hi[:, :])
        p.dma("gpsimd", ag_in[128:160, :], ssm_lo[:, :])

    p.new_phase()

    def _a2a(e):
        with allow_ici_all_to_all():
            return e.collective_compute(
                "AllToAll", ALU.bypass, replica_groups=[list(range(NC8))],
                ins=[drh["a2a_in"].ap().opt()], outs=[drh["a2a_out"].ap().opt()])
    p.add("gpsimd", _a2a)
    p.add("gpsimd", lambda e: e.collective_compute(
        "AllGather", ALU.bypass, replica_groups=[list(range(NC8))],
        ins=[drh["ag_in"].ap().opt()], outs=[drh["ag_out"].ap().opt()]))

    p.new_phase()
    with contextlib.ExitStack() as ph:
        psb = sball(ph)
        xt4 = [psb(f"sxt{q}", [128, TL], BF16) for q in range(4)]
        tsf = psb("tsf", [128, TL])
        tsb = psb("tsb", [128, TL], BF16)
        bcf = psb("bcf", [32, TL])
        bcb = psb("bcb", [32, TL], BF16)
        Bb = [psb(f"Bb{s}", [128, TL], BF16) for s in range(16)]
        Cb = [psb(f"Cb{s}", [128, TL], BF16) for s in range(16)]
        dtq = psb("dtq", [128, TL])
        dte = psb("dte", [128, TL])
        uq = psb("uq", [128, TL], BF16)
        dA = psb("dA", [128, TL])
        uB = psb("uB", [128, TL], BF16)
        Stt = [psb(f"St{s}", [128, TL], BF16) for s in range(16)]
        dt_ps = ph.enter_context(nc.psum_tensor("dt_ps", [128, TL], F32))

        for c in range(8):
            for q in range(4):
                p.dma("sync", xt4[q][:], a2a_out[c, q, :, :])
            p.dma("sync", tsf[:, :], ag_out[c, 0:128, :])
            p.add("vector", lambda e: e.tensor_copy(out=tsb[:], in_=tsf[:]))
            p.dma("sync", bcf[:, :], ag_out[c, 128:160, :])
            p.add("vector", lambda e: e.tensor_copy(out=bcb[:], in_=bcf[:]))
            p.dma("gpsimd", bcrows[:, :], bcb[:, :])
            for s in range(16):
                p.dma("sync", Bb[s][:], bcrows[s, :][None, :].broadcast_to([128, TL]))
                p.dma("sync", Cb[s][:], bcrows[16 + s, :][None, :].broadcast_to([128, TL]))
            for q in range(4):
                for c0 in (0, 512):
                    p.add("tensor", lambda e, q=q, c0=c0: e.matmul(
                        dt_ps[:, c0:c0 + 512], wdt_sl[:, q * 128:(q + 1) * 128],
                        tsb[:, c0:c0 + 512], start=True, stop=True))
                p.add("scalar", lambda e, q=q: e.activation(
                    out=dte[:], in_=dt_ps[:], func=AF.Exp, bias=dtb_sl[:, q:q + 1]))
                p.add("scalar", lambda e: e.activation(
                    out=dtq[:], in_=dte[:], func=AF.Ln, bias=1.0))
                p.add("vector", lambda e, q=q: e.tensor_mul(
                    out=uq[:], in0=dtq[:], in1=xt4[q][:]))
                for s in range(16):
                    p.add("scalar", lambda e, q=q, s=s: e.activation(
                        out=dA[:], in_=dtq[:], func=AF.Exp,
                        scale=a_sl[:, q * 16 + s:q * 16 + s + 1]))
                    p.add("vector", lambda e, s=s: e.tensor_mul(
                        out=uB[:], in0=uq[:], in1=Bb[s][:]))
                    p.add("vector", lambda e, q=q, s=s: e.tensor_tensor_scan(
                        out=Stt[s][:], data0=dA[:], data1=uB[:],
                        initial=carry[:, q * 16 + s:q * 16 + s + 1],
                        op0=ALU.mult, op1=ALU.add))
                    p.add("vector", lambda e, q=q, s=s: e.tensor_copy(
                        out=carry[:, q * 16 + s:q * 16 + s + 1],
                        in_=Stt[s][:, TL - 1:TL]))
                    p.add("vector", lambda e, s=s: e.tensor_mul(
                        out=Stt[s][:], in0=Stt[s][:], in1=Cb[s][:]))
                for stp in (8, 4, 2, 1):
                    for s in range(stp):
                        p.add("vector", lambda e, s=s, stp=stp: e.tensor_add(
                            out=Stt[s][:], in0=Stt[s][:], in1=Stt[s + stp][:]))
                p.dma("gpsimd", a2y_in[c, q, :, :], Stt[0][:, :])

    p.new_phase()

    def _a2y(e):
        with allow_ici_all_to_all():
            return e.collective_compute(
                "AllToAll", ALU.bypass, replica_groups=[list(range(NC8))],
                ins=[drh["a2y_in"].ap().opt()], outs=[drh["a2y_out"].ap().opt()])
    p.add("gpsimd", _a2y)

    p.new_phase()
    with contextlib.ExitStack() as ph5:
        psb5 = sball(ph5)
        h2T = [psb5(f"h2T{i}", [128, TL], BF16) for i in range(16)]

        with contextlib.ExitStack() as ph:
            psb = sball(ph)
            yf = [psb(f"yf{i}", [128, TL], BF16) for i in range(8)]
            ytmp = psb("ytmp", [128, TL], BF16)
            sgl = psb("sgl", [128, TL], BF16)
            wosb = [psb(f"wosb{i}", [128, 2048], BF16) for i in range(8)]
            x2acc = [psb(f"x2acc{i}", [128, D]) for i in range(8)]
            scratch = psb("scratch", [128, 2048])
            op_ps = ph.enter_context(nc.psum_tensor("op_ps", [128, D], F32))
            for kg in range(4):
                for kk in range(8):
                    k = kg * 8 + kk
                    p.dma("sync", ytmp[:], a2y_out[k // 4, k % 4, :, :])
                    p.dma("sync", yf[kk][:], a2a_in[k // 4, k % 4, :, :])
                    p.dma("sync", sgl[:], sg_dram[k, :, :])
                    p.add("vector", lambda e, kk=kk, k=k: e.scalar_tensor_tensor(
                        out=yf[kk][:], in0=yf[kk][:], scalar=d_sl[:, k:k + 1],
                        in1=ytmp[:], op0=ALU.mult, op1=ALU.add))
                    p.add("vector", lambda e, kk=kk: e.tensor_mul(
                        out=yf[kk][:], in0=yf[kk][:], in1=sgl[:]))
                    p.dma("sync", scratch[:, :], wout_t[k, :, :])
                    p.add("vector", lambda e, kk=kk: e.tensor_copy(
                        out=wosb[kk][:], in_=scratch[:]))
                for tt in range(8):
                    for kk in range(8):
                        for c in range(4):
                            p.add("tensor", lambda e, kk=kk, c=c, tt=tt: e.matmul(
                                op_ps[:, c * 512:(c + 1) * 512],
                                yf[kk][:, tt * 128:(tt + 1) * 128],
                                wosb[kk][:, c * 512:(c + 1) * 512],
                                start=(kk == 0), stop=(kk == 7)))
                    if kg == 0:
                        p.add("vector", lambda e, tt=tt: e.tensor_copy(
                            out=x2acc[tt][:], in_=op_ps[:]))
                    else:
                        p.add("vector", lambda e, tt=tt: e.tensor_add(
                            out=x2acc[tt][:], in0=x2acc[tt][:], in1=op_ps[:]))

            gab = psb("gab", [128, D], BF16)
            cb1b = psb("cb1b", [128, D], BF16)
            sbb = psb("sbb", [128, D], BF16)
            mrow = psb("mrow", [1, D], BF16)
            for nm, tile, off in (("ga", gab, 2 * D), ("cb1", cb1b, 4 * D),
                                  ("sb", sbb, 3 * D)):
                p.dma("sync", scratch[0:1, 0:D], modv[0, off:off + D][None, :])
                p.add("vector", lambda e: e.tensor_copy(
                    out=mrow[:], in_=scratch[0:1, 0:D]))
                p.dma("gpsimd", rowbf_dram[:, :], mrow[:, :])
                p.dma("sync", tile[:, :], rowbf_dram[0, :][None, :].broadcast_to([128, D]))
            sq2 = psb("sq2", [128, D])
            stat2 = psb("stat2", [128, 4])
            h2tm = psb("h2tm", [128, D], BF16)
            for tt in range(8):
                p.dma("sync", scratch[:, :], xs[HALO + tt * 128:HALO + (tt + 1) * 128, :])
                p.add("vector", lambda e, tt=tt: e.tensor_mul(
                    out=x2acc[tt][:], in0=x2acc[tt][:], in1=gab[:]))
                p.add("vector", lambda e, tt=tt: e.tensor_add(
                    out=x2acc[tt][:], in0=x2acc[tt][:], in1=scratch[:]))
                p.dma("gpsimd", x2_dram[tt, :, :], x2acc[tt][:, :])
                p.add("vector", lambda e, tt=tt: e.tensor_reduce(
                    out=stat2[:, 0:1], in_=x2acc[tt][:], axis=AXX, op=ALU.add))
                p.add("scalar", lambda e, tt=tt: e.square(out=sq2[:], in_=x2acc[tt][:]))
                p.add("vector", lambda e: e.tensor_reduce(
                    out=stat2[:, 1:2], in_=sq2[:], axis=AXX, op=ALU.add))
                p.add("vector", lambda e: e.tensor_scalar_mul(
                    out=stat2[:, 2:3], in0=stat2[:, 0:1], scalar1=1.0 / D))
                p.add("scalar", lambda e: e.square(out=stat2[:, 0:1], in_=stat2[:, 2:3]))
                p.add("vector", lambda e: e.scalar_tensor_tensor(
                    out=stat2[:, 1:2], in0=stat2[:, 1:2], scalar=1.0 / D,
                    in1=stat2[:, 0:1], op0=ALU.mult, op1=ALU.subtract))
                p.add("scalar", lambda e: e.activation(
                    out=stat2[:, 3:4], in_=stat2[:, 1:2], func=AF.Sqrt, bias=consts[:, 0:1]))
                p.add("vector", lambda e: e.reciprocal(
                    out=stat2[:, 3:4], in_=stat2[:, 3:4]))
                p.add("vector", lambda e, tt=tt: e.tensor_scalar(
                    out=x2acc[tt][:], in0=x2acc[tt][:], scalar1=stat2[:, 2:3],
                    scalar2=stat2[:, 3:4], op0=ALU.subtract, op1=ALU.mult))
                p.add("vector", lambda e, tt=tt: e.tensor_mul(
                    out=x2acc[tt][:], in0=x2acc[tt][:], in1=cb1b[:]))
                p.add("vector", lambda e, tt=tt: e.tensor_add(
                    out=h2tm[:], in0=x2acc[tt][:], in1=sbb[:]))
                for di in range(16):
                    p.dmaT("sync", h2T[di][:, tt * 128:(tt + 1) * 128],
                           h2tm[:, di * 128:(di + 1) * 128])

        with contextlib.ExitStack() as ph:
            psb = sball(ph)
            w1f = psb("w1f", [128, 2048])
            w1b = psb("w1b", [128, 2048], BF16)
            w2b = [psb(f"w2b{i}", [128, 2048], BF16) for i in range(8)]
            af = [psb(f"af{i}", [128, TL], BF16) for i in range(16)]
            q10 = psb("q10", [128, 512])
            qi = psb("qi", [128, 512], I32)
            qf = psb("qf", [128, 512])
            ffnacc = [psb(f"ffnacc{i}", [128, TL]) for i in range(16)]
            f_ps = ph.enter_context(nc.psum_tensor("f_ps", [128, TL], F32))
            e_ps = ph.enter_context(nc.psum_tensor("e_ps", [128, TL], F32))
            for fb in range(4):
                for fi in range(16):
                    f = fb * 16 + fi
                    p.dma("sync", w1f[:, :], w1_t[f, :, :])
                    p.add("vector", lambda e: e.tensor_copy(out=w1b[:], in_=w1f[:]))
                    for k in range(16):
                        for c0 in (0, 512):
                            p.add("tensor", lambda e, k=k, c0=c0: e.matmul(
                                f_ps[:, c0:c0 + 512], w1b[:, k * 128:(k + 1) * 128],
                                h2T[k][:, c0:c0 + 512], start=(k == 0), stop=(k == 15)))
                    for c0 in (0, 512):
                        p.add("scalar", lambda e, c0=c0: e.activation(
                            out=q10[:], in_=f_ps[:, c0:c0 + 512], func=AF.Copy,
                            scale=consts[:, 1:2]))
                        p.add("vector", lambda e: e.tensor_copy(out=qi[:], in_=q10[:]))
                        p.add("vector", lambda e: e.tensor_copy(out=qf[:], in_=qi[:]))
                        p.add("vector", lambda e: e.tensor_scalar(
                            out=qf[:], in0=qf[:], scalar1=32767.0, scalar2=-32768.0,
                            op0=ALU.min, op1=ALU.max))
                        p.add("scalar", lambda e, fi=fi, c0=c0: e.activation(
                            out=af[fi][:, c0:c0 + 512], in_=qf[:],
                            func=AF.Gelu_apprx_tanh, scale=consts[:, 2:3]))
                for fsub in range(2):
                    for kk in range(8):
                        k = fb * 16 + fsub * 8 + kk
                        p.dma("sync", w1f[:, :], w2_t[k, :, :])
                        p.add("vector", lambda e, kk=kk: e.tensor_copy(
                            out=w2b[kk][:], in_=w1f[:]))
                    for et in range(16):
                        for kk in range(8):
                            fi = fsub * 8 + kk
                            for c0 in (0, 512):
                                p.add("tensor", lambda e, kk=kk, fi=fi, et=et, c0=c0:
                                      e.matmul(
                                          e_ps[:, c0:c0 + 512],
                                          w2b[kk][:, et * 128:(et + 1) * 128],
                                          af[fi][:, c0:c0 + 512],
                                          start=(kk == 0), stop=(kk == 7)))
                        if fb == 0 and fsub == 0:
                            p.add("vector", lambda e, et=et: e.tensor_copy(
                                out=ffnacc[et][:], in_=e_ps[:]))
                        else:
                            p.add("vector", lambda e, et=et: e.tensor_add(
                                out=ffnacc[et][:], in0=ffnacc[et][:], in1=e_ps[:]))

            gbb = psb("gbb", [128, D], BF16)
            mrow2 = psb("mrow2", [1, D], BF16)
            p.dma("sync", w1f[0:1, 0:D], modv[0, 5 * D:6 * D][None, :])
            p.add("vector", lambda e: e.tensor_copy(out=mrow2[:], in_=w1f[0:1, 0:D]))
            p.dma("gpsimd", rowbf_dram[:, :], mrow2[:, :])
            p.dma("sync", gbb[:, :], rowbf_dram[0, :][None, :].broadcast_to([128, D]))
            for et in range(16):
                p.add("vector", lambda e, et=et: e.tensor_copy(
                    out=af[et][:], in_=ffnacc[et][:]))
            ffnT = psb("ffnT", [128, D], BF16)
            otile = psb("otile", [128, D])
            x2r = w1f
            for tt in range(8):
                for et in range(16):
                    p.dmaT("sync", ffnT[:, et * 128:(et + 1) * 128],
                           af[et][:, tt * 128:(tt + 1) * 128])
                p.dma("sync", x2r[:, :], x2_dram[tt, :, :])
                p.add("vector", lambda e: e.tensor_mul(
                    out=otile[:], in0=ffnT[:], in1=gbb[:]))
                p.add("vector", lambda e: e.tensor_add(
                    out=otile[:], in0=otile[:], in1=x2r[:]))
                p.dma("gpsimd", out[tt * 128:(tt + 1) * 128, :], otile[:, :])

    if debug:
        p.new_phase()
        taps = [("dbg_modv", "modv"), ("dbg_xt", "a2a_in"), ("dbg_ssm", "ag_in"),
                ("dbg_yraw", "a2y_in"), ("dbg_x2", "x2_dram"), ("dbg_sg", "sg_dram"),
                ("dbg_yback", "a2y_out")]
        for oname, iname in taps:
            h = drh[iname]
            t = nc.dram_tensor(oname, list(h.shape), h.dtype, kind="ExternalOutput").ap()
            p.dma("gpsimd", t[...][tuple(slice(None) for _ in h.shape)] if False else t, h.ap())
    with nc.allow_non_contiguous_dma(reason="small strided parameter loads"):
        p.emit(block)
    st.close()
    return nc


_CACHE = {}


def kernel(L0_attn_hidden_states, L0_attn_in_proj_w, L0_attn_conv_w, L0_attn_conv_b,
           L0_attn_x_proj_w, L0_attn_dt_proj_w, L0_attn_dt_proj_b, L0_attn_A,
           L0_attn_D, L0_attn_out_proj_w, L0_ffn_W1, L0_ffn_W2, L0_c, L0_wmod):
    x = np.ascontiguousarray(np.asarray(L0_attn_hidden_states, dtype=np.float32)[0])
    Win = np.asarray(L0_attn_in_proj_w, dtype=np.float32)
    cwf = np.asarray(L0_attn_conv_w, dtype=np.float32)
    cbf = np.asarray(L0_attn_conv_b, dtype=np.float32)
    Wx = np.asarray(L0_attn_x_proj_w, dtype=np.float32)
    Wdt = np.asarray(L0_attn_dt_proj_w, dtype=np.float32)
    dtbf = np.asarray(L0_attn_dt_proj_b, dtype=np.float32)
    Af = np.asarray(L0_attn_A, dtype=np.float32)
    Df = np.asarray(L0_attn_D, dtype=np.float32)
    Wout = np.asarray(L0_attn_out_proj_w, dtype=np.float32)
    W1 = np.asarray(L0_ffn_W1, dtype=np.float32)
    W2 = np.asarray(L0_ffn_W2, dtype=np.float32)
    cv = np.asarray(L0_c, dtype=np.float32).reshape(1, D)
    wmod = np.asarray(L0_wmod, dtype=np.float32)

    if "nc" not in _CACHE:
        _CACHE["nc"] = build_program()
    nc = _CACHE["nc"]
    in_maps = make_inmaps(x, Win, cwf, cbf, Wx, Wdt, dtbf, Af, Df, Wout, W1, W2, cv, wmod)
    res = run_bass_kernel_spmd(nc, in_maps, core_ids=list(range(NC8)))
    outs = [res.results[j]["out"] for j in range(NC8)]
    return np.concatenate(outs, axis=0).reshape(1, T, D)


def make_inmaps(x, Win, cwf, cbf, Wx, Wdt, dtbf, Af, Df, Wout, W1, W2, cv, wmod):
    win_t = np.ascontiguousarray(
        Win.reshape(64, 128, 16, 128).transpose(0, 3, 2, 1).reshape(64, 128, 2048))
    wx_t = np.ascontiguousarray(Wx.T.reshape(32, 128, 160))
    wout_t = np.ascontiguousarray(Wout.T.reshape(32, 128, 2048))
    w1_t = np.ascontiguousarray(
        W1.reshape(64, 128, 16, 128).transpose(0, 3, 2, 1).reshape(64, 128, 2048))
    w2_t = np.ascontiguousarray(
        W2.reshape(16, 128, 64, 128).transpose(2, 3, 0, 1).reshape(64, 128, 2048))
    cw_t = np.ascontiguousarray(cwf.reshape(32, 128, 4))
    cb_t = cbf.reshape(32, 128, 1)
    d_t = Df.reshape(32, 128, 1)
    in_maps = []
    for j in range(NC8):
        t0 = j * TL
        if j == 0:
            xsj = np.concatenate([np.zeros((HALO, D), np.float32), x[0:TL]], axis=0)
        else:
            xsj = x[t0 - HALO:t0 + TL]
        dsl = slice(j * 512, (j + 1) * 512)
        wdt_j = np.ascontiguousarray(Wdt[dsl].reshape(4, 128, 128).transpose(0, 2, 1))
        in_maps.append({
            "xs": np.ascontiguousarray(xsj),
            "win_t": win_t,
            "cw": cw_t, "cb": cb_t,
            "wx_t": wx_t,
            "wdt": wdt_j,
            "dtb": dtbf[dsl].reshape(4, 128, 1),
            "Ain": np.ascontiguousarray(Af[dsl].reshape(4, 128, 16)),
            "Din": d_t,
            "wout_t": wout_t,
            "w1_t": w1_t, "w2_t": w2_t,
            "cvec": cv,
            "wmod_j": np.ascontiguousarray(
                wmod[:, j * 1536:(j + 1) * 1536].reshape(16, 128, 1536)),
            "mask3": (np.zeros((128, 3), np.float32) if j == 0
                      else np.ones((128, 3), np.float32)),
        })
    return in_maps


# version 12
# speedup vs baseline: 2.1370x; 2.1370x over previous
import numpy as np
import concourse.bass as bass
import concourse.mybir as mybir
from concourse.bass_utils import run_bass_kernel_spmd
from concourse.aot_env import allow_ici_all_to_all

F32 = mybir.dt.float32
BF16 = mybir.dt.bfloat16
I32 = mybir.dt.int32
U8 = mybir.dt.uint8
AF = mybir.ActivationFunctionType
ALU = mybir.AluOpType
AXX = mybir.AxisListType.X

NC8 = 8
T = 8192
D = 2048
DI = 4096
DFF = 8192
TL = 1024
HALO = 128
TH = TL + HALO
EPS = 1e-5

ENGS = ("sync", "scalar", "vector", "tensor", "gpsimd")
SER = {"p0": False, "p1": False, "p3": False, "p5": False}


STREAMS = tuple(ENGS) + tuple(e + "_d" for e in ENGS) + ("sync_t", "sync_b", "coll")


class P:

    def __init__(self, nc, stack):
        self.nc = nc
        self.ops = {e: [] for e in ENGS}
        self.cum = {s: 0 for s in STREAMS}
        self.sems = {s: stack.enter_context(nc.semaphore(f"sem_{s}"))
                     for s in STREAMS}
        self.bar = []

    serial = False

    def add(self, eng, fn, deps=(), dma=False, stream=None):
        if stream is None:
            stream = (eng + "_d") if dma else eng
        inc = 16 if dma else 1
        dl = list(self.bar) + [d for d in deps if d is not None]
        if self.serial and getattr(self, "_prev", None) is not None:
            dl.append(self._prev)
        self.ops[eng].append((fn, inc, dl, stream, eng))
        self.cum[stream] += inc
        h = (stream, self.cum[stream])
        self._prev = h
        return h

    def dma(self, eng, out, in_, deps=(), stream=None):
        return self.add(eng, lambda e: e.dma_start(out=out, in_=in_), deps,
                        dma=True, stream=stream)

    def dmaT(self, eng, out, in_, deps=()):
        return self.add(eng, lambda e: e.dma_start_transpose(out, in_), deps,
                        dma=True, stream="sync_t")

    def barrier(self):
        self.bar = [(s, self.cum[s]) for s in STREAMS if self.cum[s] > 0]

    def emit(self, block):
        sems = self.sems
        ops = self.ops
        finals = dict(self.cum)

        def mk(name):
            def body(e):
                water = {s: 0 for s in STREAMS}
                for fn, inc, deps, stream, _eng in ops[name]:
                    need = {}
                    for (ds, dc) in deps:
                        if ds == name:
                            continue
                        if dc > water[ds]:
                            need[ds] = max(need.get(ds, 0), dc)
                    for ds, dc in need.items():
                        e.wait_ge(sems[ds], dc)
                        water[ds] = dc
                    fn(e).then_inc(sems[stream], inc)
                if name == "gpsimd":
                    for s in STREAMS:
                        if finals[s] > 0:
                            e.wait_ge(sems[s], finals[s])
            return body

        for en in ENGS:
            getattr(block, en)(mk(en))


def build_program(debug=False):
    import contextlib
    nc = bass.Bass("TRN2", target_bir_lowering=False, debug=False)

    def inp(name, shape, dt=F32):
        return nc.dram_tensor(name, list(shape), dt, kind="ExternalInput").ap()

    xs = inp("xs", [TH, D])
    win_t = inp("win_t", [64, 128, 2048])
    cw = inp("cw", [32, 128, 4])
    cb = inp("cb", [32, 128, 1])
    wx_t = inp("wx_t", [32, 128, 160])
    wdt = inp("wdt", [4, 128, 128])
    dtb = inp("dtb", [4, 128, 1])
    Ain = inp("Ain", [4, 128, 16])
    Din = inp("Din", [32, 128, 1])
    wout_t = inp("wout_t", [32, 128, 2048])
    w1_t = inp("w1_t", [64, 128, 2048])
    w2_t = inp("w2_t", [64, 128, 2048])
    cvec = inp("cvec", [1, D])
    wmod_j = inp("wmod_j", [16, 128, 1536])
    mask3 = inp("mask3", [128, 3])

    out = nc.dram_tensor("out", [TL, D], F32, kind="ExternalOutput").ap()

    drh = {}

    def idram(name, shape, dt=F32):
        drh[name] = nc.dram_tensor(name, list(shape), dt)
        return drh[name].ap()

    agm_in = idram("agm_in", [1, 1536])
    agm_out = idram("agm_out", [8, 1536])
    a2a_in = idram("a2a_in", [8, 4, 128, TL], BF16)
    a2a_out = idram("a2a_out", [8, 4, 128, TL], BF16)
    ag_in = idram("ag_in", [160, TL])
    ag_out = idram("ag_out", [8, 160, TL])
    a2y_in = idram("a2y_in", [8, 4, 128, TL], BF16)
    a2y_out = idram("a2y_out", [8, 4, 128, TL], BF16)
    sg_dram = idram("sg_dram", [32, 128, TL], BF16)
    x2_dram = idram("x2_dram", [8, 128, D])
    bcrows = idram("bcrows", [32, TL], BF16)
    modv = idram("modv", [1, 12288])
    sc_dram = idram("sc_dram", [1, D])
    rowbf_dram = idram("rowbf_dram", [1, D], BF16)

    st = contextlib.ExitStack()
    p = P(nc, st)
    p.serial = SER["p0"]

    def sball(stack):
        def f(name, shape, dt=F32):
            return stack.enter_context(nc.sbuf_tensor(name, list(shape), dt))
        return f

    sbp = sball(st)
    a_sl = sbp("a_sl", [128, 64])
    dtb_sl = sbp("dtb_sl", [128, 4])
    d_sl = sbp("d_sl", [128, 32])
    cw_sl = sbp("cw_sl", [128, 128])
    cb_sl = sbp("cb_sl", [128, 32])
    mask_sl = sbp("mask_sl", [128, 3], BF16)
    carry = sbp("carry", [128, 64])
    wdt_sl = sbp("wdt_sl", [128, 512], BF16)
    consts = sbp("consts", [128, 4])

    block = st.enter_context(nc.Block())

    with contextlib.ExitStack() as ph:
        psb = sball(ph)
        f0 = psb("f0", [128, 520])
        modrow = psb("modrow", [1, 12288])
        hD = p.dma("sync", d_sl[:, :].rearrange("p (q o) -> p q o", q=32),
                   Din[:, :, :].transpose([1, 0, 2]))
        p.dma("sync", a_sl[:, :].rearrange("p (q s) -> p q s", q=4),
              Ain[:, :, :].transpose([1, 0, 2]))
        p.dma("sync", dtb_sl[:, :].rearrange("p (q o) -> p q o", q=4),
              dtb[:, :, :].transpose([1, 0, 2]))
        p.dma("sync", cw_sl[:, :].rearrange("p (q k) -> p q k", q=32),
              cw[:, :, :].transpose([1, 0, 2]))
        p.dma("sync", cb_sl[:, :].rearrange("p (q o) -> p q o", q=32),
              cb[:, :, :].transpose([1, 0, 2]))
        hm = p.dma("sync", f0[:, 0:3], mask3[:, :])
        p.add("vector", lambda e: e.tensor_copy(out=mask_sl[:], in_=f0[:, 0:3]), [hm])
        p.add("vector", lambda e: e.memset(carry[:], 0.0))
        p.add("vector", lambda e: e.memset(consts[:, 0:1], EPS))
        p.add("vector", lambda e: e.memset(consts[:, 1:2], 10.0))
        p.add("vector", lambda e: e.memset(consts[:, 2:3], 0.1))
        hw = p.dma("sync", f0[:, 8:520].rearrange("p (q r) -> p q r", q=4),
                   wdt[:, :, :].transpose([1, 0, 2]))
        hwc = p.add("vector", lambda e: e.tensor_copy(out=wdt_sl[:], in_=f0[:, 8:520]),
                    [hw])

        r0 = psb("r0", [1, D])
        r1 = psb("r1", [1, D])
        m0 = psb("m0", [1, D], U8)
        sccols = psb("sccols", [128, 16])
        h1 = p.dma("sync", r0[:, :], cvec[:, :])
        h2 = p.add("scalar", lambda e: e.activation(out=r1[:], in_=r0[:], func=AF.Silu),
                   [h1])
        h3 = p.dma("gpsimd", sc_dram[:, :], r1[:, :], [h2])
        h4 = p.dma("sync", sccols[:, :], sc_dram[0, :].rearrange("(k p) -> p k", p=128),
                   [h3])

        wm_f = [psb(f"wm_f{i}", [128, 1536]) for i in range(2)]
        mod_ps = ph.enter_context(nc.psum_tensor("mod_ps", [1, 1536], F32))
        modslice = psb("modslice", [1, 1536])
        hmm = None
        hmm_hist = {}
        for k in range(16):
            hld = p.dma("sync", wm_f[k % 2][:, :], wmod_j[k, :, :],
                        [hmm_hist.get(k - 2)])
            for c in range(3):
                hmm = p.add("tensor", lambda e, k=k, c=c: e.matmul(
                    mod_ps[:, c * 512:(c + 1) * 512], sccols[:, k:k + 1],
                    wm_f[k % 2][:, c * 512:(c + 1) * 512],
                    start=(k == 0), stop=(k == 15)), [hld, h4] if c == 0 else [])
            hmm_hist[k] = hmm
        hms = p.add("vector", lambda e: e.tensor_copy(out=modslice[:], in_=mod_ps[:]),
                    [hmm])
        hst = p.dma("gpsimd", agm_in[:, :], modslice[:, :], [hms])
        hag = p.add("gpsimd", lambda e: e.collective_compute(
            "AllGather", ALU.bypass, replica_groups=[list(range(NC8))],
            ins=[drh["agm_in"].ap().opt()], outs=[drh["agm_out"].ap().opt()]), [hst],
            stream="coll")
        hmr = p.dma("sync", modrow[:, :],
                    agm_out[:, :].rearrange("a b -> (a b)")[None, :], [hag])
        hcl = hmr
        for b0 in (2 * D, 5 * D):
            ha = p.add("vector", lambda e, b0=b0: e.tensor_single_scalar(
                out=m0[:], in_=modrow[:, b0:b0 + D], scalar=0.0, op=ALU.is_ge),
                [hmr, hcl])
            hb = p.add("vector", lambda e, b0=b0: e.tensor_scalar_max(
                out=r0[:], in0=modrow[:, b0:b0 + D], scalar1=0.25), [hmr])
            hc = p.add("vector", lambda e, b0=b0: e.tensor_scalar_min(
                out=r1[:], in0=modrow[:, b0:b0 + D], scalar1=-0.25), [hmr, h2])
            hcl = p.add("vector", lambda e, b0=b0: e.select(
                out=modrow[:, b0:b0 + D], mask=m0[:], on_true=r0[:], on_false=r1[:]),
                [ha, hb, hc])
        for b0 in (D, 4 * D):
            hcl = p.add("vector", lambda e, b0=b0: e.tensor_scalar_add(
                out=modrow[:, b0:b0 + D], in0=modrow[:, b0:b0 + D], scalar1=1.0), [hcl])
        p.dma("gpsimd", modv[:, :], modrow[:, :], [hcl])

    p.barrier()
    p.serial = SER["p1"]
    with contextlib.ExitStack() as ph:
        psb = sball(ph)
        ca1b = psb("ca1b", [128, D])
        sab = psb("sab", [128, D])
        hca = p.dma("sync", ca1b[:, :], modv[0, D:2 * D][None, :].broadcast_to([128, D]), stream="sync_b")
        hsa = p.dma("sync", sab[:, :], modv[0, 0:D][None, :].broadcast_to([128, D]), stream="sync_b")

        hT = [psb(f"hT{i}", [128, TH], BF16) for i in range(16)]
        xtile = [psb(f"xtile{i}", [128, D]) for i in range(2)]
        sqt0 = psb("sqt0", [128, D]); sqt = [sqt0, sqt0]
        stat = [psb(f"stat{i}", [128, 4]) for i in range(2)]
        xhat0 = psb("xhat0", [128, D]); xhat = [xhat0, xhat0]
        htm = [psb(f"htm{i}", [128, D], BF16) for i in range(2)]
        h_tile_done = {}
        h_dmaT = {}
        h_sq_war = {}
        for tt in range(9):
            b = tt % 2
            hx = p.dma("sync", xtile[b][:, :], xs[tt * 128:(tt + 1) * 128, :],
                       [h_tile_done.get(tt - 2)])
            hr1 = p.add("vector", lambda e, b=b: e.tensor_reduce(
                out=stat[b][:, 0:1], in_=xtile[b][:], axis=AXX, op=ALU.add),
                [hx, h_dmaT.get(tt - 2)])
            hq = p.add("scalar", lambda e, b=b: e.square(out=sqt[b][:], in_=xtile[b][:]),
                       [hx, h_tile_done.get(tt - 2), h_sq_war.get(tt - 1)])
            hr2 = p.add("vector", lambda e, b=b: e.tensor_reduce(
                out=stat[b][:, 1:2], in_=sqt[b][:], axis=AXX, op=ALU.add), [hq])
            h_sq_war[tt] = hr2
            hmu = p.add("vector", lambda e, b=b: e.tensor_scalar_mul(
                out=stat[b][:, 2:3], in0=stat[b][:, 0:1], scalar1=1.0 / D), [hr1])
            hm2 = p.add("scalar", lambda e, b=b: e.square(
                out=stat[b][:, 0:1], in_=stat[b][:, 2:3]), [hmu])
            hvv = p.add("vector", lambda e, b=b: e.scalar_tensor_tensor(
                out=stat[b][:, 1:2], in0=stat[b][:, 1:2], scalar=1.0 / D,
                in1=stat[b][:, 0:1], op0=ALU.mult, op1=ALU.subtract), [hr2, hm2])
            hsq = p.add("scalar", lambda e, b=b: e.activation(
                out=stat[b][:, 3:4], in_=stat[b][:, 1:2], func=AF.Sqrt,
                bias=consts[:, 0:1]), [hvv])
            hrc = p.add("vector", lambda e, b=b: e.reciprocal(
                out=stat[b][:, 3:4], in_=stat[b][:, 3:4]), [hsq])
            hxh = p.add("vector", lambda e, b=b: e.tensor_scalar(
                out=xhat[b][:], in0=xtile[b][:], scalar1=stat[b][:, 2:3],
                scalar2=stat[b][:, 3:4], op0=ALU.subtract, op1=ALU.mult), [hrc])
            hxm = p.add("vector", lambda e, b=b: e.tensor_mul(
                out=xhat[b][:], in0=xhat[b][:], in1=ca1b[:]), [hxh, hca])
            hh = p.add("vector", lambda e, b=b: e.tensor_add(
                out=htm[b][:], in0=xhat[b][:], in1=sab[:]),
                [hxm, hsa, h_dmaT.get(tt - 2)])
            h_tile_done[tt] = hh
            last = None
            for di in range(16):
                last = p.dmaT("sync", hT[di][:, tt * 128:(tt + 1) * 128],
                              htm[b][:, di * 128:(di + 1) * 128], [hh])
            h_dmaT[tt] = last
        h_hT = None
        for di in range(16):
            h_hT = p.add("vector", lambda e, di=di: e.tensor_mul(
                out=hT[di][:, HALO - 3:HALO], in0=hT[di][:, HALO - 3:HALO],
                in1=mask_sl[:]), [h_dmaT[8]])

        wraw = [psb(f"wraw{i}", [128, 2048]) for i in range(2)]
        wbf = [psb(f"wbf{i}", [128, 2048], BF16) for i in range(2)]
        ips = contextlib.ExitStack()
        ipp = [[ips.enter_context(nc.psum_tensor(f"ip{i}_{par}", [128, 512], F32))
                for i in range(3)] for par in range(2)]
        FDC = [(0, 512), (512, 1024), (1024, TH)]
        xh = [psb(f"xh{i}", [128, TH], BF16) for i in range(2)]
        xc0 = psb("xc0", [128, TL], BF16)
        xc1 = psb("xc1", [128, TL], BF16)
        xt_t = [psb(f"xt{i}", [128, TL], BF16) for i in range(32)]
        sgt = [psb(f"sgt{i}", [128, TL], BF16) for i in range(2)]

        h_wdma = {}
        h_cast = {}
        h_mm_stop = {}
        h_evac = {}
        h_cons = {}
        h_silu = {}
        for et in range(64):
            b = et % 2
            h_wdma[et] = p.dma("sync", wraw[b][:, :], win_t[et, :, :],
                               [h_cast.get(et - 2)])
            h_cast[et] = p.add("scalar", lambda e, b=b: e.activation(
                out=wbf[b][:], in_=wraw[b][:], func=AF.Copy),
                [h_wdma[et], h_mm_stop.get(et - 2)])
            hmmi = None
            for k in range(16):
                deps0 = [h_cast[et], h_hT, h_evac.get(et - 2)] if k == 0 else []
                for ci, (c0, c1) in enumerate(FDC):
                    hmmi = p.add("tensor", lambda e, k=k, ci=ci, c0=c0, c1=c1, b=b:
                                 e.matmul(
                                     ipp[b][ci][:, 0:c1 - c0],
                                     wbf[b][:, k * 128:(k + 1) * 128],
                                     hT[k][:, c0:c1], start=(k == 0), stop=(k == 15)),
                                 deps0 if ci == 0 else [])
            h_mm_stop[et] = hmmi
            if et < 32:
                hevs = []
                for ci, (c0, c1) in enumerate(FDC):
                    hevs.append(p.add("scalar", lambda e, ci=ci, c0=c0, c1=c1, b=b:
                                e.activation(out=xh[b][:, c0:c1],
                                             in_=ipp[b][ci][:, 0:c1 - c0],
                                             func=AF.Copy),
                                [h_mm_stop[et], h_cons.get(et - 2)]))
                h_evac[et] = hevs[-1]
                hc1 = p.add("vector", lambda e, et=et, b=b: e.tensor_scalar_mul(
                    out=xc0[:], in0=xh[b][:, HALO - 3:HALO - 3 + TL],
                    scalar1=cw_sl[:, 4 * et:4 * et + 1]), hevs)
                hc2 = p.add("vector", lambda e, et=et, b=b: e.scalar_tensor_tensor(
                    out=xc1[:], in0=xh[b][:, HALO - 2:HALO - 2 + TL],
                    scalar=cw_sl[:, 4 * et + 1:4 * et + 2], in1=xc0[:],
                    op0=ALU.mult, op1=ALU.add), [hc1])
                hc3 = p.add("vector", lambda e, et=et, b=b: e.scalar_tensor_tensor(
                    out=xc0[:], in0=xh[b][:, HALO - 1:HALO - 1 + TL],
                    scalar=cw_sl[:, 4 * et + 2:4 * et + 3], in1=xc1[:],
                    op0=ALU.mult, op1=ALU.add), [hc2])
                hc4 = p.add("vector", lambda e, et=et, b=b: e.scalar_tensor_tensor(
                    out=xc1[:], in0=xh[b][:, HALO:HALO + TL],
                    scalar=cw_sl[:, 4 * et + 3:4 * et + 4], in1=xc0[:],
                    op0=ALU.mult, op1=ALU.add), [hc3])
                h_cons[et] = hc4
                h_silu[et] = p.add("scalar", lambda e, et=et: e.activation(
                    out=xt_t[et][:], in_=xc1[:], func=AF.Silu,
                    bias=cb_sl[:, et:et + 1]), [hc4])
                p.dma("gpsimd", a2a_in[et // 4, et % 4, :, :], xt_t[et][:, :],
                      [h_silu[et]])
            else:
                hg1 = p.add("scalar", lambda e, b=b: e.activation(
                    out=sgt[b][:, 0:384], in_=ipp[b][0][:, 128:512], func=AF.Silu),
                    [h_mm_stop[et], h_cons.get(et - 2)])
                hg2 = p.add("scalar", lambda e, b=b: e.activation(
                    out=sgt[b][:, 384:896], in_=ipp[b][1][:, 0:512], func=AF.Silu), [])
                hg3 = p.add("scalar", lambda e, b=b: e.activation(
                    out=sgt[b][:, 896:1024], in_=ipp[b][2][:, 0:128], func=AF.Silu), [])
                h_evac[et] = hg3
                h_cons[et] = p.dma("gpsimd", sg_dram[et - 32, :, :], sgt[b][:, :], [hg3])

        wxs = [psb(f"wxs{i}", [128, 160]) for i in range(2)]
        wxb = [psb(f"wxb{i}", [128, 160], BF16) for i in range(2)]
        ips.close()
        xp0 = ph.enter_context(nc.psum_tensor("xp0", [128, TL], F32))
        xp1 = ph.enter_context(nc.psum_tensor("xp1", [32, TL], F32))
        ssm_hi = psb("ssm_hi", [128, TL])
        ssm_lo = psb("ssm_lo", [32, TL])
        hxc = {}
        hxmm = {}
        hxm = None
        for k in range(32):
            b = k % 2
            hd = p.dma("sync", wxs[b][:, :], wx_t[k, :, :], [hxc.get(k - 2)])
            hxc[k] = p.add("scalar", lambda e, b=b: e.activation(
                out=wxb[b][:], in_=wxs[b][:], func=AF.Copy), [hd, hxmm.get(k - 2)])
            for c0 in (0, 512):
                deps0 = [hxc[k], h_silu[k]]
                if k == 0:
                    deps0 += [h_evac[62], h_evac[63]]
                p.add("tensor", lambda e, k=k, c0=c0, b=b: e.matmul(
                    xp0[:, c0:c0 + 512], wxb[b][:, 0:128], xt_t[k][:, c0:c0 + 512],
                    start=(k == 0), stop=(k == 31)), deps0 if c0 == 0 else [])
                hxm = p.add("tensor", lambda e, k=k, c0=c0, b=b: e.matmul(
                    xp1[:, c0:c0 + 512], wxb[b][:, 128:160], xt_t[k][:, c0:c0 + 512],
                    start=(k == 0), stop=(k == 31)), [])
            hxmm[k] = hxm
        he1 = p.add("vector", lambda e: e.tensor_copy(out=ssm_hi[:], in_=xp0[:]), [hxm])
        he2 = p.add("vector", lambda e: e.tensor_copy(out=ssm_lo[:], in_=xp1[:]), [hxm])
        p.dma("gpsimd", ag_in[0:128, :], ssm_hi[:, :], [he1])
        p.dma("gpsimd", ag_in[128:160, :], ssm_lo[:, :], [he2])

    p.barrier()
    def _a2a(e):
        with allow_ici_all_to_all():
            return e.collective_compute(
                "AllToAll", ALU.bypass, replica_groups=[list(range(NC8))],
                ins=[drh["a2a_in"].ap().opt()], outs=[drh["a2a_out"].ap().opt()])
    p.add("gpsimd", _a2a, stream="coll")
    p.add("gpsimd", lambda e: e.collective_compute(
        "AllGather", ALU.bypass, replica_groups=[list(range(NC8))],
        ins=[drh["ag_in"].ap().opt()], outs=[drh["ag_out"].ap().opt()]), stream="coll")

    p.barrier()
    p.serial = SER["p3"]
    with contextlib.ExitStack() as ph:
        psb = sball(ph)
        xt4 = [psb(f"sxt{q}", [128, TL], BF16) for q in range(4)]
        tsf = psb("tsf", [128, TL])
        tsb = psb("tsb", [128, TL], BF16)
        bcf = psb("bcf", [32, TL])
        bcb = psb("bcb", [32, TL], BF16)
        Ball = psb("Ball", [128, 16 * TL], BF16)
        Call = psb("Call", [128, 16 * TL], BF16)
        dtq = [psb(f"dtq{i}", [128, TL]) for i in range(2)]
        uq = [psb(f"uq{i}", [128, TL], BF16) for i in range(2)]
        dA = [psb(f"dA{i}", [128, TL]) for i in range(2)]
        uBall0 = psb("uBall0", [128, 16 * TL], BF16)
        uBall = [uBall0, uBall0]
        Stt = [psb(f"StA{par}", [128, 16 * TL], BF16) for par in range(2)]
        dte = dA[0]
        dt_ps = [ph.enter_context(nc.psum_tensor(f"dt_ps{i}", [128, TL], F32))
                 for i in range(2)]

        h_y_store = {}
        h_xt_load = {}
        h_B_load = {}
        h_C_load = {}
        h_scan = {}
        h_uB = {}
        h_dA = {}
        h_CSt = {}
        h_carry = {}
        h_uqh = {}
        h_dtq = {}
        h_bcb_st = {}
        h_tsb = {}

        def prev_q(c, q, n=1):
            idx = c * 4 + q - n
            return (idx // 4, idx % 4) if idx >= 0 else None

        for c in range(8):
            for q in range(4):
                pq = prev_q(c, q, 0)
            for q in range(4):
                h_xt_load[(c, q)] = p.dma(
                    "sync", xt4[q][:], a2a_out[c, q, :, :],
                    [h_uqh.get((c - 1, q))])
            htl = p.dma("sync", tsf[:, :], ag_out[c, 0:128, :], [h_tsb.get(c - 1)])
            h_tsb[c] = p.add("vector", lambda e: e.tensor_copy(out=tsb[:], in_=tsf[:]),
                             [htl, h_dtq.get((c - 1, 3))])
            hbl = p.dma("sync", bcf[:, :], ag_out[c, 128:160, :], [h_bcb_st.get(c - 1)])
            hbc = p.add("vector", lambda e: e.tensor_copy(out=bcb[:], in_=bcf[:]), [hbl])
            h_bcb_st[c] = p.dma("gpsimd", bcrows[:, :], bcb[:, :],
                                [hbc, h_B_load.get((c - 1, 15)),
                                 h_C_load.get((c - 1, 15))])
            for s in range(16):
                h_B_load[(c, s)] = p.dma(
                    "sync", Ball[:, s * TL:(s + 1) * TL],
                    bcrows[s, :][None, :].broadcast_to([128, TL]),
                    [h_bcb_st[c], h_uB.get((c - 1, 3))], stream="sync_b")
                h_C_load[(c, s)] = p.dma(
                    "sync", Call[:, s * TL:(s + 1) * TL],
                    bcrows[16 + s, :][None, :].broadcast_to([128, TL]),
                    [h_bcb_st[c], h_CSt.get((c - 1, 3))], stream="sync_b")
            for q in range(4):
                bq = q % 2
                qp2 = prev_q(c, q, 2)
                hm1 = p.add("tensor", lambda e, q=q, bq=bq: e.matmul(
                    dt_ps[bq][:, 0:512], wdt_sl[:, q * 128:(q + 1) * 128],
                    tsb[:, 0:512], start=True, stop=True),
                    [h_tsb[c], hwc, h_dtq.get(qp2)])
                hm2 = p.add("tensor", lambda e, q=q, bq=bq: e.matmul(
                    dt_ps[bq][:, 512:1024], wdt_sl[:, q * 128:(q + 1) * 128],
                    tsb[:, 512:1024], start=True, stop=True), [])
                hde = p.add("scalar", lambda e, q=q, bq=bq: e.activation(
                    out=dte[:], in_=dt_ps[bq][:], func=AF.Exp,
                    bias=dtb_sl[:, q:q + 1]),
                    [hm1, hm2, h_uqh.get(qp2),
                     (h_scan.get(prev_q(c, q, 1) + (14,))
                      if prev_q(c, q, 1) else None)])
                h_dtq[(c, q)] = p.add("scalar", lambda e, bq=bq: e.activation(
                    out=dtq[bq][:], in_=dte[:], func=AF.Ln, bias=1.0), [hde])
                h_uqh[(c, q)] = p.add("vector", lambda e, q=q, bq=bq: e.tensor_mul(
                    out=uq[bq][:], in0=dtq[bq][:], in1=xt4[q][:]),
                    [h_dtq[(c, q)], h_xt_load[(c, q)],
                     (h_uB.get(qp2) if qp2 else None)])
                qp1 = prev_q(c, q, 1)
                qp2t = prev_q(c, q, 2)
                h_uB[(c, q)] = p.add(
                    "vector", lambda e, bq=bq: e.tensor_mul(
                        out=uBall[bq][:],
                        in0=uq[bq][:][:, None, :].broadcast_to([128, 16, TL]),
                        in1=Ball[:].rearrange("p (s t) -> p s t", s=16)),
                    [h_uqh[(c, q)], h_B_load[(c, 15)],
                     (h_scan.get(qp1 + (15,)) if qp1 else None)])
                for s in range(16):
                    bs = s % 2
                    war_buf = (h_scan.get((c, q, s - 2)) if s >= 2
                               else (h_scan.get(qp1 + (14 + s,)) if qp1 else None))
                    h_dA[(c, q, s)] = p.add(
                        "scalar", lambda e, q=q, s=s, bs=bs, bq=bq:
                        e.activation(out=dA[bs][:], in_=dtq[bq][:], func=AF.Exp,
                                     scale=a_sl[:, q * 16 + s:q * 16 + s + 1]),
                        [h_dtq[(c, q)], war_buf])
                    h_scan[(c, q, s)] = p.add(
                        "vector", lambda e, q=q, s=s, bs=bs, bq=bq:
                        e.tensor_tensor_scan(
                            out=Stt[q % 2][:, s * TL:(s + 1) * TL],
                            data0=dA[bs][:],
                            data1=uBall[bq][:, s * TL:(s + 1) * TL],
                            initial=carry[:, q * 16 + s:q * 16 + s + 1],
                            op0=ALU.mult, op1=ALU.add),
                        [h_dA[(c, q, s)], h_uB[(c, q)], h_carry.get((c - 1, q)),
                         (h_y_store.get(qp2t) if qp2t else None)])
                h_carry[(c, q)] = p.add(
                    "vector", lambda e, q=q: e.tensor_copy(
                        out=carry[:, q * 16:(q + 1) * 16],
                        in_=Stt[q % 2][:, TL - 1:16 * TL:TL]),
                    [h_scan[(c, q, 15)]])
                h_CSt[(c, q)] = p.add(
                    "vector", lambda e, q=q: e.tensor_mul(
                        out=Stt[q % 2][:], in0=Stt[q % 2][:], in1=Call[:]),
                    [h_scan[(c, q, 15)], h_carry[(c, q)], h_C_load[(c, 15)]])
                ht1 = p.add("vector", lambda e, q=q: e.tensor_add(
                    out=Stt[q % 2][:, 0:8 * TL], in0=Stt[q % 2][:, 0:8 * TL],
                    in1=Stt[q % 2][:, 8 * TL:16 * TL]), [h_CSt[(c, q)]])
                ht2 = p.add("vector", lambda e, q=q: e.tensor_add(
                    out=Stt[q % 2][:, 0:4 * TL], in0=Stt[q % 2][:, 0:4 * TL],
                    in1=Stt[q % 2][:, 4 * TL:8 * TL]), [ht1])
                ht3 = p.add("vector", lambda e, q=q: e.tensor_add(
                    out=Stt[q % 2][:, 0:2 * TL], in0=Stt[q % 2][:, 0:2 * TL],
                    in1=Stt[q % 2][:, 2 * TL:4 * TL]), [ht2])
                htree = p.add("vector", lambda e, q=q: e.tensor_add(
                    out=Stt[q % 2][:, 0:TL], in0=Stt[q % 2][:, 0:TL],
                    in1=Stt[q % 2][:, TL:2 * TL]), [ht3])
                h_y_store[(c, q)] = p.dma("gpsimd", a2y_in[c, q, :, :],
                                          Stt[q % 2][:, 0:TL], [htree])

    p.serial = SER["p5"]
    p.barrier()
    def _a2y(e):
        with allow_ici_all_to_all():
            return e.collective_compute(
                "AllToAll", ALU.bypass, replica_groups=[list(range(NC8))],
                ins=[drh["a2y_in"].ap().opt()], outs=[drh["a2y_out"].ap().opt()])
    p.add("gpsimd", _a2y, stream="coll")

    p.barrier()
    with contextlib.ExitStack() as ph5:
        psb5 = sball(ph5)
        h2T = [psb5(f"h2T{i}", [128, TL], BF16) for i in range(16)]

        with contextlib.ExitStack() as ph:
            psb = sball(ph)
            yf = [psb(f"yf{i}", [128, TL], BF16) for i in range(8)]
            ytmp0 = psb("ytmp0", [128, TL], BF16); ytmp = [ytmp0, ytmp0]
            sgl0 = psb("sgl0", [128, TL], BF16); sgl = [sgl0, sgl0]
            wosb = [psb(f"wosb{i}", [128, 2048], BF16) for i in range(8)]
            x2acc = [psb(f"x2acc{i}", [128, D]) for i in range(8)]
            scratch = psb("scratch", [128, 2048])
            wos_f = [psb(f"wosf{i}", [128, 2048]) for i in range(2)]
            ops_s = contextlib.ExitStack()
            op_ps = [ops_s.enter_context(nc.psum_tensor(f"op_ps{i}", [128, D], F32))
                     for i in range(2)]
            h_ypost = {}
            h_wcast = {}
            h_ev = {}
            for kg in range(4):
                for kk in range(8):
                    k = kg * 8 + kk
                    bk = kk % 2
                    hy = p.dma("sync", ytmp[bk][:], a2y_out[k // 4, k % 4, :, :],
                               [h_ypost.get(k - 1)])
                    hx = p.dma("sync", yf[kk][:], a2a_in[k // 4, k % 4, :, :],
                               [h_ev.get((kg - 1, 7))])
                    hs = p.dma("sync", sgl[bk][:], sg_dram[k, :, :],
                               [h_ypost.get(k - 1)])
                    h1 = p.add("vector", lambda e, kk=kk, k=k, bk=bk:
                               e.scalar_tensor_tensor(
                                   out=yf[kk][:], in0=yf[kk][:],
                                   scalar=d_sl[:, k:k + 1], in1=ytmp[bk][:],
                                   op0=ALU.mult, op1=ALU.add), [hy, hx, hD])
                    h_ypost[k] = p.add("vector", lambda e, kk=kk, bk=bk: e.tensor_mul(
                        out=yf[kk][:], in0=yf[kk][:], in1=sgl[bk][:]), [h1, hs])
                    hwd = p.dma("sync", wos_f[bk][:, :], wout_t[k, :, :],
                                [h_wcast.get(k - 2)])
                    h_wcast[k] = p.add("scalar", lambda e, kk=kk, bk=bk: e.activation(
                        out=wosb[kk][:], in_=wos_f[bk][:], func=AF.Copy),
                        [hwd, h_ev.get((kg - 1, 7))])
                for tt in range(8):
                    bt = tt % 2
                    hlast = None
                    for kk in range(8):
                        deps0 = [h_ypost[kg * 8 + kk], h_wcast[kg * 8 + kk]]
                        if kk == 0:
                            deps0.append(h_ev.get((kg - 1, tt)))
                        for cc in range(4):
                            hlast = p.add(
                                "tensor", lambda e, kk=kk, cc=cc, tt=tt, bt=bt:
                                e.matmul(
                                    op_ps[bt][:, cc * 512:(cc + 1) * 512],
                                    yf[kk][:, tt * 128:(tt + 1) * 128],
                                    wosb[kk][:, cc * 512:(cc + 1) * 512],
                                    start=(kk == 0), stop=(kk == 7)),
                                deps0 if cc == 0 else [])
                    if kg == 0:
                        h_ev[(kg, tt)] = p.add(
                            "vector", lambda e, tt=tt, bt=bt: e.tensor_copy(
                                out=x2acc[tt][:], in_=op_ps[bt][:]), [hlast])
                    else:
                        h_ev[(kg, tt)] = p.add(
                            "vector", lambda e, tt=tt, bt=bt: e.tensor_add(
                                out=x2acc[tt][:], in0=x2acc[tt][:],
                                in1=op_ps[bt][:]), [hlast])

            gab = psb("gab", [128, D], BF16)
            cb1b = psb("cb1b", [128, D], BF16)
            sbb = psb("sbb", [128, D], BF16)
            hrow_dep = None
            hbt = {}
            for nm, tile, off in (("ga", gab, 2 * D), ("cb1", cb1b, 4 * D),
                                  ("sb", sbb, 3 * D)):
                hl = p.dma("sync", scratch[0:1, 0:D], modv[0, off:off + D][None, :],
                           [hrow_dep])
                hc = p.add("vector", lambda e, tile=tile: e.tensor_copy(
                    out=tile[0:1, :], in_=scratch[0:1, 0:D]), [hl, hrow_dep])
                hs = p.dma("gpsimd", rowbf_dram[:, :], tile[0:1, :], [hc])
                hbt[nm] = p.dma("sync", tile[:, :],
                                rowbf_dram[0, :][None, :].broadcast_to([128, D]),
                                [hs], stream="sync_b")
                hrow_dep = hbt[nm]
            sq2 = psb("sq2", [128, D])
            stat2 = psb("stat2", [128, 4])
            h2tm = [psb(f"h2tm{i}", [128, D], BF16) for i in range(2)]
            h_x2s = {}
            h_h2T = {}
            for tt in range(8):
                bt = tt % 2
                hxr = p.dma("sync", scratch[:, :],
                            xs[HALO + tt * 128:HALO + (tt + 1) * 128, :],
                            [h_ev[(3, tt)], hbt["sb"] if tt == 0 else None,
                             h_x2s.get(tt - 1)])
                hg = p.add("vector", lambda e, tt=tt: e.tensor_mul(
                    out=x2acc[tt][:], in0=x2acc[tt][:], in1=gab[:]),
                    [h_ev[(3, tt)], hbt["ga"]])
                hadd = p.add("vector", lambda e, tt=tt: e.tensor_add(
                    out=x2acc[tt][:], in0=x2acc[tt][:], in1=scratch[:]), [hg, hxr])
                h_x2s[tt] = p.dma("gpsimd", x2_dram[tt, :, :], x2acc[tt][:, :], [hadd])
                hr1 = p.add("vector", lambda e, tt=tt: e.tensor_reduce(
                    out=stat2[:, 0:1], in_=x2acc[tt][:], axis=AXX, op=ALU.add), [hadd])
                hq = p.add("scalar", lambda e, tt=tt: e.square(
                    out=sq2[:], in_=x2acc[tt][:]), [hadd])
                hr2 = p.add("vector", lambda e: e.tensor_reduce(
                    out=stat2[:, 1:2], in_=sq2[:], axis=AXX, op=ALU.add), [hq])
                hmu = p.add("vector", lambda e: e.tensor_scalar_mul(
                    out=stat2[:, 2:3], in0=stat2[:, 0:1], scalar1=1.0 / D), [hr1])
                hm2 = p.add("scalar", lambda e: e.square(
                    out=stat2[:, 0:1], in_=stat2[:, 2:3]), [hmu])
                hvv = p.add("vector", lambda e: e.scalar_tensor_tensor(
                    out=stat2[:, 1:2], in0=stat2[:, 1:2], scalar=1.0 / D,
                    in1=stat2[:, 0:1], op0=ALU.mult, op1=ALU.subtract), [hr2, hm2])
                hsq = p.add("scalar", lambda e: e.activation(
                    out=stat2[:, 3:4], in_=stat2[:, 1:2], func=AF.Sqrt,
                    bias=consts[:, 0:1]), [hvv])
                hrc = p.add("vector", lambda e: e.reciprocal(
                    out=stat2[:, 3:4], in_=stat2[:, 3:4]), [hsq])
                hxh = p.add("vector", lambda e, tt=tt: e.tensor_scalar(
                    out=x2acc[tt][:], in0=x2acc[tt][:], scalar1=stat2[:, 2:3],
                    scalar2=stat2[:, 3:4], op0=ALU.subtract, op1=ALU.mult),
                    [hrc, h_x2s[tt]])
                hxm = p.add("vector", lambda e, tt=tt: e.tensor_mul(
                    out=x2acc[tt][:], in0=x2acc[tt][:], in1=cb1b[:]),
                    [hxh, hbt["cb1"]])
                hh2 = p.add("vector", lambda e, tt=tt, bt=bt: e.tensor_add(
                    out=h2tm[bt][:], in0=x2acc[tt][:], in1=sbb[:]),
                    [hxm, hbt["sb"], h_h2T.get(tt - 2)])
                hlast = None
                for di in range(16):
                    hlast = p.dmaT("sync", h2T[di][:, tt * 128:(tt + 1) * 128],
                                   h2tm[bt][:, di * 128:(di + 1) * 128], [hh2])
                h_h2T[tt] = hlast
            h_h2_all = h_h2T[7]
            ops_s.close()

        with contextlib.ExitStack() as ph:
            psb = sball(ph)
            af = [psb(f"af{i}", [128, TL], BF16) for i in range(16)]
            ffnacc = [psb(f"ffnacc{i}", [128, TL]) for i in range(16)]
            gbb = psb("gbb", [128, D], BF16)
            phw = contextlib.ExitStack()
            psw = sball(phw)
            w1f = [psw(f"w1f{i}", [128, 2048]) for i in range(2)]
            w1b = [psw(f"w1b{i}", [128, 2048], BF16) for i in range(2)]
            w2f0 = psw("w2f0", [128, 2048]); w2f = [w2f0, w2f0]
            w2b = [psw(f"w2b{i}", [128, 2048], BF16) for i in range(8)]
            q10a = psw("q10a", [128, 512]); q10 = [q10a, q10a]
            qia = psw("qia", [128, 512], I32); qi = [qia, qia]
            qfa = psw("qfa", [128, 512]); qf = [qfa, qfa]
            f_ps = [phw.enter_context(nc.psum_tensor(f"f_ps{i}", [128, TL], F32))
                    for i in range(2)]
            e_ps = [phw.enter_context(nc.psum_tensor(f"e_ps{i}", [128, TL], F32))
                    for i in range(2)]
            h_w1c = {}
            h_gelu = {}
            h_q_done = {}
            h_acc = {}
            h_w2c = {}
            h_w2_mm = {}
            for fb in range(4):
                for fi in range(16):
                    f = fb * 16 + fi
                    bf_ = f % 2
                    hwd = p.dma("sync", w1f[bf_][:, :], w1_t[f, :, :],
                                [h_w1c.get(f - 2)])
                    h_w1c[f] = p.add("scalar", lambda e, bf_=bf_: e.activation(
                        out=w1b[bf_][:], in_=w1f[bf_][:], func=AF.Copy),
                        [hwd, h_w2_mm.get((fb - 1, 1)) if fi == 0 else None])
                    hmmf = None
                    for k in range(16):
                        deps0 = ([h_w1c[f], h_h2_all, h_q_done.get(f - 2)]
                                 if k == 0 else [])
                        for c0 in (0, 512):
                            hmmf = p.add("tensor", lambda e, k=k, c0=c0, bf_=bf_:
                                         e.matmul(
                                             f_ps[bf_][:, c0:c0 + 512],
                                             w1b[bf_][:, k * 128:(k + 1) * 128],
                                             h2T[k][:, c0:c0 + 512],
                                             start=(k == 0), stop=(k == 15)),
                                         deps0 if c0 == 0 else [])
                    for ci, c0 in enumerate((0, 512)):
                        bq = ci
                        hq1 = p.add("scalar", lambda e, c0=c0, bf_=bf_, bq=bq:
                                    e.activation(out=q10[bq][:],
                                                 in_=f_ps[bf_][:, c0:c0 + 512],
                                                 func=AF.Copy, scale=consts[:, 1:2]),
                                    [hmmf, h_gelu.get((f - 2, ci)),
                                     h_gelu.get((f, ci - 1), h_gelu.get((f - 1, 1))),
                                     h_w2_mm.get((fb - 1, 1))
                                     if fi == 0 and ci == 0 else None])
                        hq2 = p.add("vector", lambda e, bq=bq: e.tensor_copy(
                            out=qi[bq][:], in_=q10[bq][:]), [hq1])
                        hq3 = p.add("vector", lambda e, bq=bq: e.tensor_copy(
                            out=qf[bq][:], in_=qi[bq][:]), [hq2])
                        hq4 = p.add("vector", lambda e, bq=bq: e.tensor_scalar(
                            out=qf[bq][:], in0=qf[bq][:], scalar1=32767.0,
                            scalar2=-32768.0, op0=ALU.min, op1=ALU.max), [hq3])
                        h_gelu[(f, ci)] = p.add(
                            "scalar", lambda e, fi=fi, c0=c0, bq=bq: e.activation(
                                out=af[fi][:, c0:c0 + 512], in_=qf[bq][:],
                                func=AF.Gelu_apprx_tanh, scale=consts[:, 2:3]), [hq4])
                    h_q_done[f] = h_gelu[(f, 1)]
                for fsub in range(2):
                    for kk in range(8):
                        k = fb * 16 + fsub * 8 + kk
                        bw = kk % 2
                        hwd = p.dma("sync", w2f[bw][:, :], w2_t[k, :, :],
                                    [h_w2c.get((fb, fsub, kk - 1)),
                                     h_w2c.get((fb, fsub - 1, 7))
                                     if fsub == 1 and kk == 0 else None,
                                     h_w2c.get((fb - 1, 1, 7))
                                     if fsub == 0 and kk == 0 and fb > 0 else None,
                                     h_w2c.get((fb, fsub - 1, 6 + kk))
                                     if fsub == 1 and kk < 2 else None,
                                     h_w2c.get((fb - 1, 1, 6 + kk))
                                     if fsub == 0 and kk < 2 else None])
                        h_w2c[(fb, fsub, kk)] = p.add(
                            "scalar", lambda e, kk=kk, bw=bw: e.activation(
                                out=w2b[kk][:], in_=w2f[bw][:], func=AF.Copy),
                            [hwd, h_w2_mm.get((fb - 1, 1))
                             if fsub == 0 and kk < 2 else None,
                             h_acc.get((fb, fsub - 1, 15))
                             if fsub == 1 and kk < 2 else None])
                    hmm2 = None
                    for et in range(16):
                        be = et % 2
                        if et >= 2:
                            war = h_acc.get((fb, fsub, et - 2))
                        elif fsub == 1:
                            war = h_acc.get((fb, 0, 14 + et))
                        elif fb > 0:
                            war = h_acc.get((fb - 1, 1, 14 + et))
                        else:
                            war = None
                        for kk in range(8):
                            fi = fsub * 8 + kk
                            deps0 = [h_w2c[(fb, fsub, kk)],
                                     h_gelu[(fb * 16 + fi, 1)]]
                            if kk == 0:
                                deps0.append(war)
                            for c0 in (0, 512):
                                hmm2 = p.add(
                                    "tensor",
                                    lambda e, kk=kk, fi=fi, et=et, c0=c0, be=be:
                                    e.matmul(
                                        e_ps[be][:, c0:c0 + 512],
                                        w2b[kk][:, et * 128:(et + 1) * 128],
                                        af[fi][:, c0:c0 + 512],
                                        start=(kk == 0), stop=(kk == 7)),
                                    deps0 if c0 == 0 else [])
                        if fb == 0 and fsub == 0:
                            h_acc[(fb, fsub, et)] = p.add(
                                "vector", lambda e, et=et, be=be: e.tensor_copy(
                                    out=ffnacc[et][:], in_=e_ps[be][:]), [hmm2])
                        else:
                            h_acc[(fb, fsub, et)] = p.add(
                                "vector", lambda e, et=et, be=be: e.tensor_add(
                                    out=ffnacc[et][:], in0=ffnacc[et][:],
                                    in1=e_ps[be][:]), [hmm2])
                    h_w2_mm[(fb, fsub)] = h_acc[(fb, fsub, 15)]

            hl = p.dma("sync", w1f[0][0:1, 0:D], modv[0, 5 * D:6 * D][None, :],
                       [h_w1c.get(63)])
            hcr = p.add("vector", lambda e: e.tensor_copy(
                out=gbb[0:1, :], in_=w1f[0][0:1, 0:D]), [hl])
            hsr = p.dma("gpsimd", rowbf_dram[:, :], gbb[0:1, :], [hcr])
            hgb = p.dma("sync", gbb[:, :],
                        rowbf_dram[0, :][None, :].broadcast_to([128, D]),
                        [hsr], stream="sync_b")
            phw.close()
            h_fcast = {}
            for et in range(16):
                h_fcast[et] = p.add("vector", lambda e, et=et: e.tensor_copy(
                    out=af[et][:], in_=ffnacc[et][:]),
                    [h_acc[(3, 1, et)], h_w2_mm[(3, 1)]])
            ffnT = [psb(f"ffnT{i}", [128, D], BF16) for i in range(2)]
            otile = [psb(f"otile{i}", [128, D]) for i in range(2)]
            x2r = [psb(f"x2r{i}", [128, D]) for i in range(2)]
            h_out = {}
            for tt in range(8):
                bt = tt % 2
                hlast = None
                for et in range(16):
                    hlast = p.dmaT("sync", ffnT[bt][:, et * 128:(et + 1) * 128],
                                   af[et][:, tt * 128:(tt + 1) * 128],
                                   [h_fcast[et], h_out.get(tt - 2)])
                hxr = p.dma("sync", x2r[bt][:, :], x2_dram[tt, :, :],
                            [h_out.get(tt - 2), h_x2s[tt]])
                hm = p.add("vector", lambda e, bt=bt: e.tensor_mul(
                    out=otile[bt][:], in0=ffnT[bt][:], in1=gbb[:]),
                    [hlast, hgb, h_out.get(tt - 2)])
                ha = p.add("vector", lambda e, bt=bt: e.tensor_add(
                    out=otile[bt][:], in0=otile[bt][:], in1=x2r[bt][:]), [hm, hxr])
                h_out[tt] = p.dma("gpsimd", out[tt * 128:(tt + 1) * 128, :],
                                  otile[bt][:, :], [ha])

    if debug:
        p.barrier()
        taps = [("dbg_modv", "modv"), ("dbg_xt", "a2a_in"), ("dbg_ssm", "ag_in"),
                ("dbg_yraw", "a2y_in"), ("dbg_x2", "x2_dram"), ("dbg_sg", "sg_dram"),
                ("dbg_yback", "a2y_out"),
                ]
        for oname, iname in taps:
            h = drh[iname]
            t = nc.dram_tensor(oname, list(h.shape), h.dtype,
                               kind="ExternalOutput").ap()
            p.dma("gpsimd", t, h.ap())

    with nc.allow_non_contiguous_dma(reason="small strided parameter loads"):
        p.emit(block)
    st.close()
    return nc


_CACHE = {}


def kernel(L0_attn_hidden_states, L0_attn_in_proj_w, L0_attn_conv_w, L0_attn_conv_b,
           L0_attn_x_proj_w, L0_attn_dt_proj_w, L0_attn_dt_proj_b, L0_attn_A,
           L0_attn_D, L0_attn_out_proj_w, L0_ffn_W1, L0_ffn_W2, L0_c, L0_wmod):
    x = np.ascontiguousarray(np.asarray(L0_attn_hidden_states, dtype=np.float32)[0])
    Win = np.asarray(L0_attn_in_proj_w, dtype=np.float32)
    cwf = np.asarray(L0_attn_conv_w, dtype=np.float32)
    cbf = np.asarray(L0_attn_conv_b, dtype=np.float32)
    Wx = np.asarray(L0_attn_x_proj_w, dtype=np.float32)
    Wdt = np.asarray(L0_attn_dt_proj_w, dtype=np.float32)
    dtbf = np.asarray(L0_attn_dt_proj_b, dtype=np.float32)
    Af = np.asarray(L0_attn_A, dtype=np.float32)
    Df = np.asarray(L0_attn_D, dtype=np.float32)
    Wout = np.asarray(L0_attn_out_proj_w, dtype=np.float32)
    W1 = np.asarray(L0_ffn_W1, dtype=np.float32)
    W2 = np.asarray(L0_ffn_W2, dtype=np.float32)
    cv = np.asarray(L0_c, dtype=np.float32).reshape(1, D)
    wmod = np.asarray(L0_wmod, dtype=np.float32)

    if "nc" not in _CACHE:
        _CACHE["nc"] = build_program()
    nc = _CACHE["nc"]
    in_maps = make_inmaps(x, Win, cwf, cbf, Wx, Wdt, dtbf, Af, Df, Wout, W1, W2,
                          cv, wmod)
    res = run_bass_kernel_spmd(nc, in_maps, core_ids=list(range(NC8)))
    outs = [res.results[j]["out"] for j in range(NC8)]
    return np.concatenate(outs, axis=0).reshape(1, T, D)


def make_inmaps(x, Win, cwf, cbf, Wx, Wdt, dtbf, Af, Df, Wout, W1, W2, cv, wmod):
    win_t = np.ascontiguousarray(
        Win.reshape(64, 128, 16, 128).transpose(0, 3, 2, 1).reshape(64, 128, 2048))
    wx_t = np.ascontiguousarray(Wx.T.reshape(32, 128, 160))
    wout_t = np.ascontiguousarray(Wout.T.reshape(32, 128, 2048))
    w1_t = np.ascontiguousarray(
        W1.reshape(64, 128, 16, 128).transpose(0, 3, 2, 1).reshape(64, 128, 2048))
    w2_t = np.ascontiguousarray(
        W2.reshape(16, 128, 64, 128).transpose(2, 3, 0, 1).reshape(64, 128, 2048))
    cw_t = np.ascontiguousarray(cwf.reshape(32, 128, 4))
    cb_t = cbf.reshape(32, 128, 1)
    d_t = Df.reshape(32, 128, 1)
    in_maps = []
    for j in range(NC8):
        t0 = j * TL
        if j == 0:
            xsj = np.concatenate([np.zeros((HALO, D), np.float32), x[0:TL]], axis=0)
        else:
            xsj = x[t0 - HALO:t0 + TL]
        dsl = slice(j * 512, (j + 1) * 512)
        wdt_j = np.ascontiguousarray(Wdt[dsl].reshape(4, 128, 128).transpose(0, 2, 1))
        in_maps.append({
            "xs": np.ascontiguousarray(xsj),
            "win_t": win_t,
            "cw": cw_t, "cb": cb_t,
            "wx_t": wx_t,
            "wdt": wdt_j,
            "dtb": dtbf[dsl].reshape(4, 128, 1),
            "Ain": np.ascontiguousarray(Af[dsl].reshape(4, 128, 16)),
            "Din": d_t,
            "wout_t": wout_t,
            "w1_t": w1_t, "w2_t": w2_t,
            "cvec": cv,
            "wmod_j": np.ascontiguousarray(
                wmod[:, j * 1536:(j + 1) * 1536].reshape(16, 128, 1536)),
            "mask3": (np.zeros((128, 3), np.float32) if j == 0
                      else np.ones((128, 3), np.float32)),
        })
    return in_maps
